# version 06 (final)
# baseline (speedup 1.0000x reference)
import numpy as np
import ml_dtypes
import concourse.bass as bass
import concourse.mybir as mybir
from concourse.bass_utils import run_bass_kernel_spmd

F32 = mybir.dt.float32
BF16 = mybir.dt.bfloat16
I32 = mybir.dt.int32
AF = mybir.ActivationFunctionType
ALU = mybir.AluOpType
AX = mybir.AxisListType
EngT = mybir.EngineType
BF = ml_dtypes.bfloat16

B, S, H = 4, 2048, 2048
T = 1024
NCH = H // 128
NH, NKV, HD = 16, 4, 128
KVD = NKV * HD
FF = 16384
FFH = 8192
INTER = 8192
QKV = 6144
NCTRL = 130156
NSPL = INTER * H
DT_SPL = (NCTRL - 1.0) / (NSPL - 1.0)
SCALE_ATT = float(HD) ** -0.5
LN_EPS = 1e-5
RMS_EPS = float(np.finfo(np.float32).eps)
NEG = -1e30
NFLAGS = 4

N_SUBBLOCKS = 4
DBG = ()
STOP_AFTER_QKV1 = False

_HOST_CACHE = {}


def host_constants(role):
    key = ("const", role)
    if key in _HOST_CACHE:
        return _HOST_CACHE[key]
    d = {}
    d["identI"] = np.eye(128, dtype=BF)
    d["ones_colI"] = np.ones((128, 1), dtype=BF)
    d["ones_rowI"] = np.ones((1, 128), dtype=np.float32)
    tl = np.arange(T)
    pos = (2 * (tl // 128) + role) * 128 + tl % 128
    inv = 1.0 / (10000.0 ** (np.arange(0, HD, 2, dtype=np.float32) / HD))
    fr = pos[:, None].astype(np.float64) * inv[None, :]
    d["cos_I"] = np.ascontiguousarray(
        np.repeat(np.cos(fr), 2, axis=-1).T).astype(BF)
    d["sin_I"] = np.ascontiguousarray(
        np.repeat(np.sin(fr), 2, axis=-1).T).astype(BF)
    R = np.zeros((128, 128), np.float32)
    for e in range(0, 128, 2):
        R[e, e + 1] = -1.0
        R[e + 1, e] = 1.0
    d["rmatTI"] = np.ascontiguousarray(R.T).astype(BF)
    dm = np.zeros((4, 128, 512), np.float32)
    for v in range(4):
        q = v * 128 + np.arange(128)[:, None]
        j = np.arange(512)[None, :]
        dm[v] = np.where(j <= q, 0.0, NEG)
    d["dmaskI"] = np.ascontiguousarray(dm[[role, 2 + role]]).astype(BF)
    u = (np.arange(128, dtype=np.float64) - 64.0) * DT_SPL
    r4 = np.stack([u ** k for k in range(4)], 0)
    d["ramp4I"] = np.concatenate([r4, r4], 0).astype(BF)
    _HOST_CACHE[key] = d
    return d


def _catmull_g(psi):
    A = np.array([
        [0.0, -0.5, 1.0, -0.5],
        [1.0, 0.0, -2.5, 1.5],
        [0.0, 0.5, 2.0, -1.5],
        [0.0, 0.0, -0.5, 0.5],
    ], dtype=np.float64)
    psi = psi.astype(np.float64)
    out = np.empty((4, 4) + psi.shape, np.float64)
    for dd in range(4):
        al, be, ga, de = A[dd]
        out[0, dd] = al + be * psi + ga * psi ** 2 + de * psi ** 3
        out[1, dd] = be + 2 * ga * psi + 3 * de * psi ** 2
        out[2, dd] = ga + 3 * de * psi
        out[3, dd] = de
    return out


def spline_meta(role, O):
    key = ("spl", role, O)
    if key in _HOST_CACHE:
        return _HOST_CACHE[key]
    I = NSPL // O
    Oh = O // 2
    nc0 = I // 128
    M = nc0 * Oh
    assert M == 65536, M
    c0s = np.repeat(np.arange(nc0, dtype=np.int64), Oh)
    olocs = np.tile(np.arange(Oh, dtype=np.int64), nc0)
    o = olocs + role * Oh
    s0 = o * I + c0s * 128
    t0 = s0.astype(np.float64) * DT_SPL
    cA = np.minimum(np.floor(t0).astype(np.int64), NCTRL - 2)
    psi = t0 - cA
    idx = np.stack([np.clip(cA - 1 + dd, 0, NCTRL - 1) for dd in range(5)], 0)
    UC = 64.0 * DT_SPL
    GA = _catmull_g(psi + UC)
    GB = _catmull_g(psi + UC - 1.0)
    GD = np.zeros((4, 5, M), np.float64)
    GD[:, 0:4, :] += GA
    GD[:, 1:5, :] -= GB
    pstar = np.ceil((1.0 - psi) / DT_SPL).astype(np.int64)
    pstar = np.where(psi + (pstar - 1) * DT_SPL >= 1.0, pstar - 1, pstar)
    pstar = np.clip(pstar, 0, 128)
    mask_full = (np.arange(128)[:, None] >= pstar[None, :])
    ntiles = M // 512
    out = {
        "idx": idx,
        "GA": np.ascontiguousarray(GA.reshape(4, 4, 128, 512)).astype(np.float32),
        "GD": np.ascontiguousarray(GD.reshape(4, 5, 128, 512)).astype(np.float32),
        "mask": np.ascontiguousarray(
            mask_full.reshape(128, ntiles, 512).transpose(1, 0, 2)).astype(BF),
        "I": I, "Oh": Oh, "nc0": nc0, "ntiles": ntiles,
    }
    _HOST_CACHE[key] = out
    return out


class Sched:
    ENGS = ("PE", "DV", "SE", "PO", "SP")

    def __init__(self, nc, sems):
        self.nc = nc
        self.sems = sems
        self.ops = {e: [] for e in self.ENGS}
        self.meta = []
        self.nops = 0

    def op(self, eng, emit, waits=(), q=None):
        oid = self.nops
        self.nops += 1
        ws = [w for w in waits if w is not None]
        self.meta.append([eng, q, False, 0])
        self.ops[eng].append((oid, emit, tuple(ws)))
        return oid

    def finalize_and_emit(self, block):
        for e in self.ENGS:
            for oid, emit, waits in self.ops[e]:
                for w in waits:
                    we, wq, _, _ = self.meta[w]
                    if wq is None and we == e:
                        continue
                    self.meta[w][2] = True
        SERIAL_ENGS = ("DV", "SE", "PO")
        for e in SERIAL_ENGS:
            for oid, emit, waits in self.ops[e]:
                if self.meta[oid][1] is None:
                    self.meta[oid][2] = True
        cnt = {}
        for e in self.ENGS:
            for oid, emit, waits in self.ops[e]:
                eng, q, marked, _ = self.meta[oid]
                if marked or q is not None:
                    sem = q if q is not None else eng
                    step = 16 if q is not None else 1
                    cnt[sem] = cnt.get(sem, 0) + step
                    self.meta[oid][3] = cnt[sem]
        self.final_counts = dict(cnt)
        secs = {"PE": block.tensor, "DV": block.vector, "SE": block.scalar,
                "PO": block.gpsimd, "SP": block.sync}

        SERIAL_QUEUES = ("QA", "QAx")
        prev_q_val = {}
        qser = {}
        for e in self.ENGS:
            for oid, emit, waits in self.ops[e]:
                q = self.meta[oid][1]
                if q in SERIAL_QUEUES:
                    if q in prev_q_val:
                        qser[oid] = prev_q_val[q]
                    prev_q_val[q] = self.meta[oid][3]

        def emit_engine(e):
            def f(engobj):
                waited = {}
                prev_own = [0]
                for oid, emit, waits in self.ops[e]:
                    if e in SERIAL_ENGS and self.meta[oid][1] is None and prev_own[0]:
                        if waited.get(e, 0) < prev_own[0]:
                            engobj.wait_ge(self.sems[e], prev_own[0])
                            waited[e] = prev_own[0]
                    if oid in qser:
                        sem = self.meta[oid][1]
                        if waited.get(sem, 0) < qser[oid]:
                            engobj.wait_ge(self.sems[sem], qser[oid])
                            waited[sem] = qser[oid]
                    for w in waits:
                        we, wq, marked, val = self.meta[w]
                        sem = wq if wq is not None else we
                        if wq is None and we == e:
                            continue
                        assert val > 0, f"op {oid} waits unmarked {w}"
                        if waited.get(sem, 0) >= val:
                            continue
                        engobj.wait_ge(self.sems[sem], val)
                        waited[sem] = val
                    inst = emit(engobj)
                    eng, q, marked, val = self.meta[oid]
                    if marked or q is not None:
                        assert inst is not None, f"op {oid} returned no inst"
                        sem = q if q is not None else eng
                        step = 16 if q is not None else 1
                        inst.then_inc(self.sems[sem], step)
                        if q is None and e in SERIAL_ENGS:
                            prev_own[0] = val
            return f
        for e in self.ENGS:
            if self.ops[e]:
                secs[e](emit_engine(e))


def build_program(nsub=None, dbg=None, race_detect=True):
    nsub = N_SUBBLOCKS if nsub is None else nsub
    dbg = DBG if dbg is None else dbg
    nc = bass.Bass("TRN2", target_bir_lowering=False,
                   detect_race_conditions=race_detect)

    def din(name, shape, dtype=F32):
        return nc.dram_tensor(name, shape, dtype, kind="ExternalInput").ap()

    xT = din("xT", [H, T])
    wq = din("wq", [H, H]); wk = din("wk", [KVD, H]); wv = din("wv", [KVD, H])
    wo = din("wo", [H, H])
    bq_c = din("bq_c", [128, NCH]); bk_c = din("bk_c", [128, NKV])
    bv_row = din("bv_row", [1, KVD]); bo_c = din("bo_c", [128, NCH])
    wgu_sl = din("wgu_sl", [FFH, H])
    wp_sl = din("wp_sl", [H // 2, FFH])
    wqkv = din("wqkv", [H, QKV])
    bqkv_c = din("bqkv_c", [128, 32])
    bqkvv_row = din("bqkvv_row", [1, H])
    wo1 = din("wo1", [H, H]); bo1_c = din("bo1_c", [128, NCH])
    cpwI = {w: din(f"cpw_{w}", [5, 128, 512]) for w in "gud"}
    gaI = {w: din(f"ga_{w}", [4, 4, 128, 512]) for w in "gud"}
    gdI = {w: din(f"gd_{w}", [4, 5, 128, 512]) for w in "gud"}
    mskI = {w: din(f"msk_{w}", [128, 128, 512], BF16) for w in "gud"}
    identI = din("identI", [128, 128], BF16)
    ones_colI = din("ones_colI", [128, 1], BF16)
    ones_rowI = din("ones_rowI", [1, 128])
    cos_I = din("cos_I", [128, T], BF16); sin_I = din("sin_I", [128, T], BF16)
    rmatTI = din("rmatTI", [128, 128], BF16)
    dmaskI = din("dmaskI", [2, 128, 512], BF16)
    ramp4I = din("ramp4I", [8, 128], BF16)
    roleI = din("roleI", [1, 2], I32)
    magicI = din("magicI", [1, 2 * NFLAGS], I32)

    yout = nc.dram_tensor("y", [H, T], F32, kind="ExternalOutput").ap()
    dbg_shapes = {
        "q0": ([H, T], BF16), "xn3": ([H, T], BF16),
        "resid0": ([H, T], F32), "resid1": ([H, T], F32),
        "resid2": ([H, T], F32),
        "wgu": ([FF, H], BF16), "wp": ([H, FFH], BF16),
        "splg": ([H, INTER], BF16), "spld": ([INTER, H], BF16),
    }
    dbg_t = {}
    for name in dbg:
        shp, dt_ = dbg_shapes[name]
        dbg_t[name] = nc.dram_tensor("dbg_" + name, shp, dt_,
                                     kind="ExternalOutput").ap()

    wq_bf = nc.dram_tensor("wq_bf", [H, H], BF16).ap()
    wk_bf = nc.dram_tensor("wk_bf", [KVD, H], BF16).ap()
    wv_bf = nc.dram_tensor("wv_bf", [KVD, H], BF16).ap()
    wo_bf = nc.dram_tensor("wo_bf", [H, H], BF16).ap()
    wqkv_bf = nc.dram_tensor("wqkv_bf", [H, QKV], BF16).ap()
    wo1_bf = nc.dram_tensor("wo1_bf", [H, H], BF16).ap()
    qdram = nc.dram_tensor("qdram", [H, T], BF16).ap()
    kpre = nc.dram_tensor("kpre", [KVD, T], BF16).ap()
    hdram = nc.dram_tensor("hdram", [FFH // 128, 128, T], BF16).ap()
    coefd = {w: nc.dram_tensor(f"coef_{w}", [12, 65536], BF16).ap() for w in "gud"}
    vloc0 = nc.dram_tensor("vloc0", [T, KVD], BF16).ap()
    kloc1 = nc.dram_tensor("kloc1", [H, T], BF16).ap()
    vloc1 = nc.dram_tensor("vloc1", [T, H], BF16).ap()
    wguloc = nc.dram_tensor("wguloc", [FFH, H], BF16).ap()
    wploc = nc.dram_tensor("wploc", [H // 2, FFH], BF16).ap()
    sploc = {"g": nc.dram_tensor("sploc_g", [H, INTER // 2], BF16).ap(),
             "u": nc.dram_tensor("sploc_u", [H, INTER // 2], BF16).ap(),
             "d": nc.dram_tensor("sploc_d", [INTER, H // 2], BF16).ap()}

    def shr(name, shape, dtype=BF16):
        return nc.dram_tensor(name, shape, dtype, addr_space="Shared").ap()
    sh_k0 = shr("sh_k0", [2, KVD, T])
    sh_v0 = shr("sh_v0", [2, T, KVD])
    sh_k1 = shr("sh_k1", [2, H, T])
    sh_v1 = shr("sh_v1", [2, T, H])
    sh_wgu = shr("sh_wgu", [FF, H])
    sh_wp = shr("sh_wp", [H, FFH])
    sh_spl = {"g": shr("sh_g", [H, INTER]), "u": shr("sh_u", [H, INTER]),
              "d": shr("sh_d", [INTER, H])}
    sh_flag = shr("sh_flag", [2, NFLAGS, 2], I32)
    flgscr = nc.dram_tensor("flgscr", [NFLAGS, 2], I32).ap()

    from contextlib import ExitStack
    ctx = ExitStack()
    with ctx:
        def sbt(name, shape, dtype=F32):
            return ctx.enter_context(nc.sbuf_tensor(name, shape, dtype))

        resid = sbt("resid", [128, NCH * T])
        xnatt = sbt("xnatt", [128, NCH * T], BF16)
        wbuf_t = [sbt(f"wbuf{i}", [128, 16 * 512], BF16) for i in range(2)]
        wbuf = [w.ap() for w in wbuf_t]
        ksbh = wbuf[0][:, 0:4096]
        vsbh = wbuf[0][:, 4096:8192]
        scr1 = sbt("scr1", [128, 4096], BF16)
        probs = scr1.ap()
        sgring = scr1.ap()
        scr2 = sbt("scr2", [128, 4096], BF16)
        probsT = scr2.ap()
        hring = scr2.ap()[:, 0:2048]
        hsqring = scr2.ap()[:, 2048:4096]
        tmp1_t = sbt("tmp1", [128, T])
        tmp1 = tmp1_t.ap()
        coefacc = tmp1[:, 0:512]
        gtmp = tmp1[:, 512:1024]
        tmp2_t = sbt("tmp2", [128, 512])
        tmp2 = tmp2_t.ap()
        mdtmp = tmp2
        cos_t = sbt("cos_t", [128, T], BF16)
        sin_t = sbt("sin_t", [128, T], BF16)
        dmask_sb = sbt("dmask_sb", [128, 2 * 512], BF16)
        qring = sbt("qring", [128, 4 * 512], BF16)
        kring = sbt("kring", [128, 4 * 512], BF16)
        bq_sb = sbt("bq_sb", [128, NCH]); bk_sb = sbt("bk_sb", [128, NKV])
        bo_sb = sbt("bo_sb", [128, NCH]); bqkv_sb = sbt("bqkv_sb", [128, 32])
        bo1_sb = sbt("bo1_sb", [128, NCH])
        bvb_sb = sbt("bvb_sb", [128, 512])
        bvrow_sb = sbt("bvrow_sb", [1, 512])
        ident_sb = sbt("ident_sb", [128, 128], BF16)
        onescol = sbt("onescol", [128, 1], BF16)
        onesrow = sbt("onesrow", [1, 128])
        rmat_sb = sbt("rmat_sb", [128, 128], BF16)
        ramp4_sb = sbt("ramp4_sb", [8, 128], BF16)
        role_sb = sbt("role_sb", [1, 2], I32)
        magic_sb = sbt("magic_sb", [1, 2 * NFLAGS], I32)
        flagpoll_sb = sbt("flagpoll_sb", [1, 2], I32)
        srow = sbt("srow", [1, T])
        rstd_sb = sbt("rstd_sb", [128, T])
        smalls = sbt("smalls", [128, 64])
        sclbf = sbt("sclbf", [128, 16], BF16)
        xsq = sbt("xsq", [128, 2 * 512], BF16)
        vring = sbt("vring", [128, 2 * 512], BF16)
        evalrhs = sbt("evalrhs", [8, 2 * 512], BF16)
        evalrhsD = sbt("evalrhsD", [4, 2 * 512], BF16)
        wqt_t = sbt("wqt", [128, 2 * 512], BF16)
        wqt = wqt_t.ap()
        rql_t = sbt("rql", [128, 1024], BF16)
        rql = rql_t.ap()
        nql_t = sbt("nql", [128, 512], BF16)
        nql = nql_t.ap()
        emask = rql
        wout = nql
        cpw_sb = sbt("cpw_sb", [128, 5 * 512])
        gbuf = sbt("gbuf", [128, 2 * 512])
        coefbf = sbt("coefbf", [128, 512], BF16)
        coefbf2 = sbt("coefbf2", [128, 512], BF16)

        def psum(name, shape, dtype=F32):
            return ctx.enter_context(nc.psum_tensor(name, shape, dtype))
        PSA = psum("PSA", [128, 512]); PSB = psum("PSB", [128, 512])
        PSC = psum("PSC", [128, 512]); PSD = psum("PSD", [128, 512])
        PSE = [psum("PSE0", [128, 1024], BF16), psum("PSE1", [128, 1024], BF16)]
        PSF = psum("PSF", [128, 512]); PSG = psum("PSG", [128, 512])
        PS4 = [PSA, PSB, PSC, PSD]
        PSNAME = {id(PSA): "A", id(PSB): "B", id(PSC): "C", id(PSD): "D",
                  id(PSE[0]): "E0", id(PSE[1]): "E1", id(PSF): "F", id(PSG): "G"}

        semnames = (["PE", "DV", "SE", "PO", "SP", "QS", "QA", "QGst", "QGf"] +
                    [f"QGc{i}" for i in range(7)] +
                    [f"QGx{i}" for i in range(10)] + ["QAx"] +
                    [f"QG{i}" for i in range(12)] +
                    [f"PL{i}" for i in range(NFLAGS)])
        sems = {n: ctx.enter_context(nc.semaphore(n)) for n in semnames}
        block = ctx.enter_context(nc.Block())
        sc = Sched(nc, sems)

        psfree = {}

        def pswait(ps):
            return psfree.get(PSNAME[id(ps)], [])

        def psrel(ps, ops):
            psfree[PSNAME[id(ps)]] = [o for o in ops if o is not None]

        _role = {}

        def role_of(e, eng):
            if eng not in _role:
                _role[eng] = e.value_load(role_sb[0:1, 0:1])
            return _role[eng]

        _oppo = {}

        def oppo_of(e, eng):
            if eng not in _oppo:
                _oppo[eng] = e.value_load(role_sb[0:1, 1:2])
            return _oppo[eng]

        _flagmine = {}

        def flag_mine(e):
            if not _flagmine:
                _flagmine["ap"] = sh_flag[bass.ts(role_of(e, "PO"), 1)].squeeze(0)
            return _flagmine["ap"]

        _flagtheirs = {}

        def flag_theirs(e):
            if not _flagtheirs:
                _flagtheirs["ap"] = sh_flag[bass.ts(oppo_of(e, "PO"), 1)].squeeze(0)
            return _flagtheirs["ap"]

        _xch = [0]

        def shared_copy(dst_fn, src_ap, waits):
            ch = f"QGx{_xch[0] % 10}"
            _xch[0] += 1

            def emit(e):
                r = role_of(e, "PO")
                return e.dma_start(out=dst_fn(r), in_=src_ap)
            return sc.op("PO", emit, waits=list(waits) + [t_role], q=ch)

        ld_x = sc.op("SE", lambda e: e.dma_start(
            out=resid.ap().rearrange("p (c t) -> p c t", c=NCH),
            in_=xT.rearrange("(c p) t -> p c t", p=128)), q="QAx")
        const_ops = []
        for dst, src in ((cos_t, cos_I), (sin_t, sin_I), (ident_sb, identI),
                         (onescol, ones_colI), (onesrow, ones_rowI),
                         (rmat_sb, rmatTI), (ramp4_sb, ramp4I),
                         (bq_sb, bq_c), (bk_sb, bk_c), (bo_sb, bo_c),
                         (bqkv_sb, bqkv_c), (bo1_sb, bo1_c),
                         (role_sb, roleI), (magic_sb, magicI)):
            const_ops.append(sc.op(
                "SE", (lambda d, s: lambda e: e.dma_start(out=d[:], in_=s))(dst, src),
                q="QA"))
        const_ops.append(sc.op("SE", lambda e: e.dma_start(
            out=dmask_sb.ap().rearrange("p (v f) -> p v f", v=2),
            in_=dmaskI.rearrange("v p f -> p v f")), q="QA"))
        t_consts = const_ops[-1]
        t_role = t_consts

        cast_ops = {}
        for ci, (nm, dst, src) in enumerate(
                (("wq", wq_bf, wq), ("wk", wk_bf, wk),
                 ("wv", wv_bf, wv), ("wo", wo_bf, wo),
                 ("wqkv", wqkv_bf, wqkv), ("wo1", wo1_bf, wo1))):
            cast_ops[nm] = sc.op(
                "PO", (lambda d, s: lambda e: e.dma_start(out=d, in_=s))(dst, src),
                q=f"QGc{ci}")

        def publish_flag(fi, waits):
            def emit(e):
                return e.dma_start(
                    out=flag_mine(e)[fi:fi + 1, :],
                    in_=magic_sb[0:1, 2 * fi:2 * fi + 2])
            return sc.op("PO", emit, waits=list(waits) + [t_role], q="QGf")

        def poll_flag(fi, waits=()):
            pl = f"PL{fi}"

            def emit(e):
                ft = flag_theirs(e)
                m0 = e.value_load(magic_sb[0:1, 2 * fi:2 * fi + 1])
                loop, end = f"poll{fi}", f"polle{fi}"
                nc.br(loop, engines=[EngT.Pool])
                with nc.body(loop, valid_engines=[EngT.Pool]):
                    e.dma_start(out=flagpoll_sb[:], in_=ft[fi:fi + 1, :]
                                ).then_inc(sems[pl], 16)
                    v0 = e.value_load(flagpoll_sb[0:1, 0:1])
                    nc.br_cmp(v0, m0, on_true=end, on_false=loop,
                              comp_op="IS_EQ", engines=[EngT.Pool])
                nc.switch_bb(end)
                return e.dma_start(out=flgscr[fi:fi + 1, :], in_=flagpoll_sb[:])
            return sc.op("PO", emit, waits=list(waits) + [t_role, t_consts], q="QGf")

        def emit_ln(extra_waits=()):
            cast = sc.op("DV", lambda e: e.tensor_copy(xnatt[:], resid[:]),
                         waits=[ld_x] + list(extra_waits))
            last_xn_of_th = None
            xn_last = None
            for th in range(2):
                sq_ops = []
                mm2_ops = []
                pfg = pswait(PSG) + ([last_xn_of_th] if last_xn_of_th else [])
                pff = pswait(PSF) + ([last_xn_of_th] if last_xn_of_th else [])
                for c in range(NCH):
                    sl = c * T + th * 512
                    w = [cast]
                    if c >= 2:
                        w.append(mm2_ops[c - 2])
                    sq_ops.append(sc.op("SE", (lambda c, sl: lambda e: e.activation(
                        xsq[:, (c % 2) * 512:(c % 2) * 512 + 512],
                        xnatt[:, sl:sl + 512], AF.Square))(c, sl), waits=w))
                    sc.op("PE", (lambda c, sl: lambda e: e.matmul(
                        PSG[0:1, :], onescol[:], xnatt[:, sl:sl + 512],
                        start=(c == 0), stop=(c == NCH - 1)))(c, sl),
                        waits=[cast, t_consts] + (pfg if c == 0 else []))
                    mm2_ops.append(sc.op("PE", (lambda c: lambda e: e.matmul(
                        PSF[0:1, :], onescol[:],
                        xsq[:, (c % 2) * 512:(c % 2) * 512 + 512],
                        start=(c == 0), stop=(c == NCH - 1)))(c),
                        waits=[sq_ops[c]] + (pff if c == 0 else [])))
                t_stats = mm2_ops[-1]
                ms = srow[0:1, 0:512]
                vs = srow[0:1, 512:1024]
                d1 = sc.op("DV", (lambda ms: lambda e: e.tensor_scalar(
                    ms, PSG[0:1, :], 1.0 / H, None, ALU.mult))(ms),
                    waits=[t_stats])
                d2 = sc.op("DV", (lambda ms: lambda e: e.tensor_mul(
                    tmp2[0:1, :], ms, ms))(ms), waits=[d1])
                d3 = sc.op("DV", (lambda vs: lambda e: e.scalar_tensor_tensor(
                    vs, PSF[0:1, :], 1.0 / H, tmp2[0:1, :], ALU.mult,
                    ALU.subtract))(vs), waits=[d2, t_stats])
                d4 = sc.op("DV", (lambda vs: lambda e: e.tensor_scalar(
                    vs, vs, LN_EPS, None, ALU.add))(vs), waits=[d3])
                d5 = sc.op("DV", (lambda vs: lambda e: e.reciprocal(vs, vs))(vs),
                           waits=[d4])
                s1 = sc.op("SE", (lambda vs: lambda e: e.activation(
                    vs, vs, AF.Sqrt))(vs), waits=[d5])
                bc1 = sc.op("PE", (lambda ms: lambda e: e.matmul(
                    PSF[:], onesrow[:], ms, start=True, stop=True))(ms),
                    waits=[d5, s1])
                bc2 = sc.op("PE", (lambda vs: lambda e: e.matmul(
                    PSG[:], onesrow[:], vs, start=True, stop=True))(vs),
                    waits=[s1])
                for c in range(NCH):
                    sl = c * T + th * 512
                    da = sc.op("DV", (lambda sl: lambda e: e.tensor_sub(
                        tmp1[:, 0:512], resid[:, sl:sl + 512], PSF[:]))(sl),
                        waits=[bc1, bc2, sq_ops[c]])
                    xn_last = sc.op("DV", (lambda sl: lambda e: e.tensor_mul(
                        xnatt[:, sl:sl + 512], tmp1[:, 0:512], PSG[:]))(sl),
                        waits=[da])
                last_xn_of_th = xn_last
            psrel(PSF, [xn_last]); psrel(PSG, [xn_last])
            return xn_last

        wbuf_owner = [None, None]

        def mm_stream(*, wsel, transpose, order, nk, rhs_ap, evac,
                      w_waits=(), rhs_waits=(), n_free=512):
            load_last = []
            pe_last_of_group = {}
            psum_free = {0: pswait(PSA), 1: pswait(PSB)}
            pend = []
            last_mm = None
            free_final = {0: [], 1: []}
            for gi, g in enumerate(order):
                lw = list(w_waits)
                if gi >= 2:
                    lw.append(pe_last_of_group[gi - 2])
                elif wbuf_owner[gi % 2] is not None:
                    lw.append(wbuf_owner[gi % 2])
                lo = None
                for kk in range(nk):
                    lo = sc.op("SP", (lambda g, kk, gi: lambda e: e.dma_start(
                        out=wbuf[gi % 2][:, kk * 512:(kk + 1) * 512],
                        in_=wsel(g, kk), transpose=transpose))(g, kk, gi),
                        waits=lw if kk == 0 else [], q="QS")
                load_last.append(lo)
                for m in range(4):
                    for th in range(2):
                        idx = gi * 8 + m * 2 + th
                        pi = idx % 2
                        pt = [PSA, PSB][pi]
                        w = [load_last[gi]] + list(rhs_waits) + psum_free[pi]
                        mm = None
                        for kk in range(nk):
                            mm = sc.op("PE", (lambda m, th, kk, gi, pt: lambda e: e.matmul(
                                pt[:, 0:n_free],
                                wbuf[gi % 2][:, kk * 512 + m * 128:kk * 512 + (m + 1) * 128],
                                rhs_ap(kk, th),
                                start=(kk == 0), stop=(kk == nk - 1)))(m, th, kk, gi, pt),
                                waits=(w if kk == 0 else []))
                        last_mm = mm
                        pend.append((g, m, th, pt, mm, pi))
                        if len(pend) > 1:
                            pg, pm, pth, ppt, pmm, ppi = pend.pop(0)
                            fr = list(evac(pg, pm, pth, ppt, pmm))
                            psum_free[ppi] = fr
                            free_final[ppi] = fr
                pe_last_of_group[gi] = last_mm
            while pend:
                pg, pm, pth, ppt, pmm, ppi = pend.pop(0)
                fr = list(evac(pg, pm, pth, ppt, pmm))
                psum_free[ppi] = fr
                free_final[ppi] = fr
            psrel(PSA, free_final[0]); psrel(PSB, free_final[1])
            ng = len(order)
            for par in range(2):
                last_gi = ng - 1 if (ng - 1) % 2 == par else ng - 2
                if last_gi >= 0:
                    wbuf_owner[par] = pe_last_of_group[last_gi]
            return last_mm

        t_xn = emit_ln()

        qk_store = []

        def make_qk_evac(kind, base):
            ring = qring if kind == "q" else kring
            st = {"i": 0}

            def evac(g, m, th, pt, t_mm):
                co = g * 4 + m
                i = st["i"]; st["i"] += 1
                slot = i % 4
                dsl = ring[:, slot * 512:(slot + 1) * 512]
                bias = (bq_sb if kind == "q" else bk_sb)[:, co:co + 1]
                w = [t_mm]
                if i >= 4:
                    w.append(qk_store[base + i - 4])
                se = sc.op("SE", (lambda dsl, pt, bias: lambda e: e.activation(
                    dsl, pt[:], AF.Identity, bias=bias))(dsl, pt, bias), waits=w)
                dst = (qdram if kind == "q" else kpre)[
                    co * 128:(co + 1) * 128, th * 512:th * 512 + 512]
                qk_store.append(sc.op("SE", (lambda dst, dsl: lambda e: e.dma_start(
                    out=dst, in_=dsl))(dst, dsl), waits=[se], q="QA"))
                return [se]
            return evac

        mm_stream(
            wsel=lambda g, kk: wq_bf[g * 512:(g + 1) * 512, kk * 128:(kk + 1) * 128],
            transpose=True, order=list(range(4)), nk=16,
            rhs_ap=lambda kk, th: xnatt[:, kk * T + th * 512:kk * T + th * 512 + 512],
            evac=make_qk_evac("q", 0), w_waits=[cast_ops["wq"]], rhs_waits=[t_xn])
        kbase = len(qk_store)
        mm_stream(
            wsel=lambda g, kk: wk_bf[g * 512:(g + 1) * 512, kk * 128:(kk + 1) * 128],
            transpose=True, order=[0], nk=16,
            rhs_ap=lambda kk, th: xnatt[:, kk * T + th * 512:kk * T + th * 512 + 512],
            evac=make_qk_evac("k", kbase), w_waits=[cast_ops["wk"]], rhs_waits=[t_xn])

        bv_ld = sc.op("SE", lambda e: e.dma_start(out=bvrow_sb[:], in_=bv_row),
                      q="QA")
        wv_last = None
        for kk in range(16):
            wv_last = sc.op("SP", (lambda kk: lambda e: e.dma_start(
                out=wbuf[0][:, kk * 512:(kk + 1) * 512],
                in_=wv_bf[0:512, kk * 128:(kk + 1) * 128], transpose=True))(kk),
                waits=([cast_ops["wv"], wbuf_owner[0]] if kk == 0 else []), q="QS")
        bvb_mm = sc.op("PE", lambda e: e.matmul(
            PSD[:], onesrow[:], bvrow_sb[0:1, :], start=True, stop=True),
            waits=[t_consts, bv_ld] + pswait(PSD))
        bvb_cp = sc.op("DV", lambda e: e.tensor_copy(bvb_sb[:], PSD[:]),
                       waits=[bvb_mm])
        v_bank_free = {0: pswait(PSC), 1: [bvb_cp]}
        v_stores = []
        v_dvv = {}
        mmv = None
        for ti in range(8):
            w = [wv_last, t_xn] + v_bank_free[ti % 2]
            for kk in range(16):
                mmv = sc.op("PE", (lambda ti, kk: lambda e: e.matmul(
                    PS4[2 + ti % 2][:],
                    xnatt[:, kk * T + ti * 128:kk * T + (ti + 1) * 128],
                    wbuf[0][:, kk * 512:(kk + 1) * 512],
                    start=(kk == 0), stop=(kk == 15)))(ti, kk),
                    waits=(w if kk == 0 else []))
            wv2 = [mmv, bvb_cp]
            if ti >= 2:
                wv2.append(v_stores[ti - 2])
            dvv = sc.op("DV", (lambda ti: lambda e: e.tensor_add(
                vring[:, (ti % 2) * 512:(ti % 2) * 512 + 512],
                PS4[2 + ti % 2][:], bvb_sb[:]))(ti), waits=wv2)
            v_dvv[ti] = dvv
            v_bank_free[ti % 2] = [dvv]
            v_stores.append(sc.op("SE", (lambda ti: lambda e: e.dma_start(
                out=vloc0[ti * 128:(ti + 1) * 128, :],
                in_=vring[:, (ti % 2) * 512:(ti % 2) * 512 + 512]))(ti),
                waits=[dvv], q="QA"))
        psrel(PSC, [v_dvv[6]]); psrel(PSD, [v_dvv[7]])
        wbuf_owner[0] = mmv

        rope_stores = []
        rot_free = {0: pswait(PSC), 1: pswait(PSD)}
        ri = 0
        for kind, nheads, base in (("q", NH, 0), ("k", NKV, kbase)):
            for co in range(nheads):
                for th in range(2):
                    slot = ri % 4
                    src = (qdram if kind == "q" else kpre)[
                        co * 128:(co + 1) * 128, th * 512:th * 512 + 512]
                    dsl = (qring if kind == "q" else kring)[
                        :, slot * 512:(slot + 1) * 512]
                    w = [qk_store[base + co * 2 + th]]
                    if ri >= 4:
                        w.append(rope_stores[ri - 4])
                    ldo = sc.op("SE", (lambda src, dsl: lambda e: e.dma_start(
                        out=dsl, in_=src))(src, dsl), waits=w, q="QA")
                    pb = [PSC, PSD][ri % 2]
                    rot = sc.op("PE", (lambda dsl, pb: lambda e: e.matmul(
                        pb[:], rmat_sb[:], dsl, start=True, stop=True))(dsl, pb),
                        waits=[ldo, t_consts] + rot_free[ri % 2])
                    cs = cos_t[:, th * 512:th * 512 + 512]
                    sn = sin_t[:, th * 512:th * 512 + 512]
                    d1 = sc.op("DV", (lambda sn, pb: lambda e: e.tensor_mul(
                        tmp1[:, 0:512], pb[:], sn))(sn, pb), waits=[rot])
                    rot_free[ri % 2] = [d1]
                    d2 = sc.op("DV", (lambda dsl, cs: lambda e: e.tensor_mul(
                        tmp1[:, 512:1024], dsl, cs))(dsl, cs), waits=[rot])
                    d3 = sc.op("DV", (lambda dsl: lambda e: e.tensor_add(
                        dsl, tmp1[:, 512:1024], tmp1[:, 0:512]))(dsl),
                        waits=[d2, d1])
                    sto = sc.op("SE", (lambda src, dsl: lambda e: e.dma_start(
                        out=src, in_=dsl))(src, dsl), waits=[d3], q="QA")
                    rope_stores.append(sto)
                    ri += 1
        psrel(PSC, rot_free[0]); psrel(PSD, rot_free[1])
        cp_k0 = shared_copy(
            lambda r: sh_k0[bass.ts(r, 1)].squeeze(0), kpre,
            [rope_stores[-1]])
        cp_v0 = shared_copy(
            lambda r: sh_v0[bass.ts(r, 1)].squeeze(0), vloc0,
            [v_stores[-1]])
        f_kv0 = publish_flag(0, [cp_k0, cp_v0])
        t_kv0 = poll_flag(0, waits=[f_kv0])

        quant_tiles = [("gu", ii, c2) for ii in range(FFH // 128)
                       for c2 in range(4)]
        quant_tiles += [("wp", ii, c2) for ii in range(8) for c2 in range(16)]
        wqt_reader = {0: None, 1: None}
        q_store_last = None
        for ti, (knd, ii, c2) in enumerate(quant_tiles):
            slot = ti % 2
            if knd == "gu":
                src = wgu_sl[ii * 128:(ii + 1) * 128, c2 * 512:(c2 + 1) * 512]
            else:
                src = wp_sl[ii * 128:(ii + 1) * 128, c2 * 512:(c2 + 1) * 512]
            w = [wqt_reader[slot]] if wqt_reader[slot] else []
            ld = sc.op("PO", (lambda src, slot: lambda e: e.dma_start(
                out=wqt[:, slot * 512:(slot + 1) * 512], in_=src))(src, slot),
                waits=w, q=f"QG{slot}")
            wv3 = wqt.rearrange("p (s g k) -> p s g k", s=2, g=8)[:, slot, :, :]
            red = sc.op("DV", (lambda wv3: lambda e: e.tensor_reduce(
                smalls[:, 0:8, None], wv3, AX.X, ALU.add,
                apply_absolute_value=True))(wv3), waits=[ld])
            scl = sc.op("DV", lambda e: e.tensor_scalar(
                sclbf[:, 0:8], smalls[:, 0:8], 1.0 / 64, 1e-8, ALU.mult, ALU.max),
                waits=[red])
            cp1 = sc.op("DV", lambda e: e.tensor_copy(smalls[:, 16:24],
                                                      sclbf[:, 0:8]),
                        waits=[scl])
            rcp = sc.op("DV", lambda e: e.reciprocal(
                smalls[:, 16:24], smalls[:, 16:24]), waits=[cp1])
            rr = sc.op("DV", (lambda wv3: lambda e: e.tensor_tensor(
                rql[:, 0:512].rearrange("p (g k) -> p g k", g=8), wv3,
                smalls[:, 16:24, None].broadcast_to([128, 8, 64]),
                ALU.mult))(wv3), waits=[rcp])
            wqt_reader[slot] = rr
            nn = sc.op("DV", lambda e: e.tensor_single_scalar(
                nql[:, 0:512], rql[:, 0:512], -0.5, ALU.is_lt), waits=[rr])
            dd = sc.op("DV", lambda e: e.scalar_tensor_tensor(
                rql[:, 0:512], rql[:, 0:512], 0.5, nql[:, 0:512],
                ALU.is_gt, ALU.subtract), waits=[nn])
            wh = sc.op("DV", lambda e: e.tensor_tensor(
                nql[:, 0:512].rearrange("p (g k) -> p g k", g=8),
                rql[:, 0:512].rearrange("p (g k) -> p g k", g=8),
                sclbf[:, 0:8, None].broadcast_to([128, 8, 64]), ALU.mult),
                waits=[dd])
            if knd == "gu":
                def emit_st(e, ii=ii, c2=c2):
                    return e.dma_start(
                        out=wguloc[ii * 128:(ii + 1) * 128,
                                   c2 * 512:(c2 + 1) * 512], in_=nql[:, 0:512])
            else:
                def emit_st(e, ii=ii, c2=c2):
                    return e.dma_start(
                        out=wploc[ii * 128:(ii + 1) * 128,
                                  c2 * 512:(c2 + 1) * 512], in_=nql[:, 0:512])
            q_store_last = sc.op("PO", emit_st, waits=[wh], q="QGst")
        cp_wgu = shared_copy(
            lambda r: sh_wgu.rearrange("(r o) c -> r o c", r=2)[
                bass.ts(r, 1)].squeeze(0), wguloc, [q_store_last])
        cp_wp = shared_copy(
            lambda r: sh_wp.rearrange("(r o) c -> r o c", r=2)[
                bass.ts(r, 1)].squeeze(0), wploc, [q_store_last])
        f_quant = publish_flag(1, [cp_wgu, cp_wp])

        def emit_attention(layer, t_ready_list, q_ready_ops, scale, wbuf_guard):
            ksrc = sh_k0 if layer == 0 else sh_k1
            vsrc = sh_v0 if layer == 0 else sh_v1
            kv_loads = {}
            kv_reader = {0: list(wbuf_guard), 1: list(wbuf_guard)}
            q_reader = {0: None, 1: None}
            bank_free = {bn: pswait(PS4[bn]) for bn in range(4)}
            pse_free = {0: pswait(PSE[0]), 1: pswait(PSE[1])}
            psf_free = pswait(PSF)
            probs_free = [None, None]
            probsT_free = None
            att_evac = None
            last_kv_reader = None
            negm_user = {0: None, 1: None}
            for h in range(NH):
                kvh = h // 4 if layer == 0 else h
                kslot = kvh % 2
                if layer == 1 or h % 4 == 0:
                    w = list(t_ready_list) + kv_reader[kslot]
                    lk = lv = None
                    for r in range(2):
                        kdst = ksbh.rearrange(
                            "p (s c r f) -> p s c r f", s=2, c=8, r=2)[
                            :, kslot, :, r, :]
                        lk = sc.op("SE", (lambda kvh, r, kdst: lambda e: e.dma_start(
                            out=kdst,
                            in_=ksrc[r, kvh * 128:(kvh + 1) * 128, :]
                            .rearrange("p (c f) -> p c f", c=8)))(kvh, r, kdst),
                            waits=(w if r == 0 else []), q="QA")
                        vdst = vsbh.rearrange(
                            "p (s c r f) -> p s c r f", s=2, c=8, r=2)[
                            :, kslot, :, r, :]
                        lv = sc.op("SE", (lambda kvh, r, vdst: lambda e: e.dma_start(
                            out=vdst,
                            in_=vsrc[r, :, kvh * 128:(kvh + 1) * 128]
                            .rearrange("(c p) f -> p c f", p=128)))(kvh, r, vdst),
                            waits=[], q="QA")
                    kv_loads[kslot] = (lk, lv)
                    kv_reader[kslot] = []
                qslot = h % 2
                wq_ = [q_ready_ops[h]]
                if q_reader[qslot] is not None:
                    wq_.append(q_reader[qslot])
                lq = sc.op("SE", (lambda h, qslot: lambda e: e.dma_start(
                    out=qring[:, qslot * 1024:(qslot + 1) * 1024],
                    in_=qdram[h * 128:(h + 1) * 128, :]))(h, qslot),
                    waits=wq_, q="QA")
                lk, lv = kv_loads[kslot]
                last_score_mm = None
                pv = None
                for qq in range(4):
                    pair_nrm = []
                    for Qi in range(2):
                        Q = 2 * qq + Qi
                        njs = Q // 2 + 1
                        mm_ops = []
                        for js in range(njs):
                            bn = js % 4
                            mm = sc.op("PE", (lambda Q, js, qslot, kslot, bn: lambda e: e.matmul(
                                PS4[bn][:],
                                qring[:, qslot * 1024 + Q * 128:qslot * 1024 + (Q + 1) * 128],
                                ksbh[:, kslot * 2048 + js * 512:kslot * 2048 + (js + 1) * 512],
                                start=True, stop=True))(Q, js, qslot, kslot, bn),
                                waits=[lq, lk] + bank_free[bn])
                            if js == njs - 1:
                                u = Q % 2
                                mm = sc.op("DV", (lambda bn, u: lambda e: e.tensor_tensor(
                                    PS4[bn][:], PS4[bn][:],
                                    dmask_sb[:, u * 512:(u + 1) * 512],
                                    ALU.add))(bn, u), waits=[mm, t_consts])
                            mm_ops.append(mm)
                            last_score_mm = mm
                        rmx = []
                        for js in range(njs):
                            rmx.append(sc.op("DV", (lambda js: lambda e: e.reduce_max(
                                smalls[:, 40 + js:41 + js],
                                PS4[js % 4][:], AX.X))(js), waits=[mm_ops[js]]))
                        cmb = sc.op("DV", (lambda njs: lambda e: e.reduce_max(
                            smalls[:, 44:45], smalls[:, 40:40 + njs], AX.X))(njs),
                            waits=rmx)
                        ncl = 45 if (Q % 2 == 0) else 52
                        ngw = [cmb]
                        if negm_user[Q % 2] is not None:
                            ngw.append(negm_user[Q % 2])
                        ngm = sc.op("DV", (lambda ncl: lambda e: e.tensor_scalar(
                            smalls[:, ncl:ncl + 1], smalls[:, 44:45], -scale, None,
                            ALU.mult))(ncl), waits=ngw)
                        exps = []
                        for js in range(njs):
                            ex = sc.op("SE", (lambda Qi, js, ncl: lambda e: e.activation(
                                probs[:, Qi * 2048 + js * 512:Qi * 2048 + (js + 1) * 512],
                                PS4[js % 4][:], AF.Exp, bias=smalls[:, ncl:ncl + 1],
                                scale=scale,
                                accum_out=smalls[:, 46 + js:47 + js]))(Qi, js, ncl),
                                waits=[ngm, mm_ops[js]] +
                                      ([probs_free[Qi]] if probs_free[Qi] else []))
                            exps.append(ex)
                            bank_free[js % 4] = [ex]
                        negm_user[Q % 2] = exps[-1]
                        dsm = sc.op("DV", (lambda njs: lambda e: e.reduce_sum(
                            smalls[:, 50:51], smalls[:, 46:46 + njs], AX.X))(njs),
                            waits=exps)
                        rcp2 = sc.op("DV", lambda e: e.reciprocal(
                            smalls[:, 51:52], smalls[:, 50:51]), waits=[dsm])
                        nrm = sc.op("DV", (lambda Qi, njs: lambda e: e.tensor_scalar(
                            probs[:, Qi * 2048:Qi * 2048 + njs * 512],
                            probs[:, Qi * 2048:Qi * 2048 + njs * 512],
                            smalls[:, 51:52], None, ALU.mult))(Qi, njs),
                            waits=[rcp2] + exps)
                        pair_nrm.append(nrm)
                    cp_ops = []
                    bi = 0
                    for Qi in range(2):
                        Q = 2 * qq + Qi
                        nj = 2 * Q + 2
                        batch = []
                        trs = []
                        for j in range(nj):
                            pe_b = PSE[bi % 2]
                            w = [pair_nrm[Qi], t_consts] + pse_free[bi % 2]
                            tr = sc.op("PE", (lambda Qi, j, pe_b: lambda e: e.transpose(
                                pe_b[:, (j % 8) * 128:(j % 8 + 1) * 128],
                                probs[:, Qi * 2048 + j * 128:Qi * 2048 + (j + 1) * 128],
                                ident_sb[:]))(Qi, j, pe_b),
                                waits=(w if len(batch) == 0 else []))
                            trs.append(tr)
                            batch.append(j)
                            if len(batch) == 8 or j == nj - 1:
                                j0, nb = batch[0], len(batch)
                                wcp = [trs[-1]]
                                if probsT_free is not None:
                                    wcp.append(probsT_free)
                                cp = sc.op("DV", (lambda Qi, j0, nb, pe_b: lambda e: e.tensor_copy(
                                    probsT.rearrange("p (j q) -> p j q", j=16)[
                                        :, j0:j0 + nb, Qi * 128:(Qi + 1) * 128],
                                    pe_b[:, 0:nb * 128].rearrange(
                                        "p (j q) -> p j q", j=nb)))(Qi, j0, nb, pe_b),
                                    waits=wcp)
                                pse_free[bi % 2] = [cp]
                                cp_ops.append(cp)
                                batch, trs = [], []
                                bi += 1
                        probs_free[Qi] = cp_ops[-1]
                    njmax = 2 * (2 * qq + 1) + 2
                    zw = cp_ops[:]
                    if probsT_free is not None:
                        zw.append(probsT_free)
                    zmem = sc.op("DV", (lambda qq: lambda e: e.memset(
                        probsT.rearrange("p (j q) -> p j q", j=16)[
                            :, 4 * qq + 2:4 * qq + 4, 0:128], 0.0))(qq),
                        waits=zw)
                    for j in range(njmax):
                        pv = sc.op("PE", (lambda j, kslot, njmax: lambda e: e.matmul(
                            PSF[:, 0:256],
                            vsbh[:, kslot * 2048 + j * 128:kslot * 2048 + (j + 1) * 128],
                            probsT[:, j * 256:(j + 1) * 256],
                            start=(j == 0), stop=(j == njmax - 1)))(j, kslot, njmax),
                            waits=(([lv, zmem] + cp_ops + psf_free) if j == 0 else []))
                    probsT_free = pv
                    att_evac = sc.op("SE", (lambda h, qq: lambda e: e.activation(
                        xnatt[:, h * T + qq * 256:h * T + (qq + 1) * 256],
                        PSF[:, 0:256], AF.Copy))(h, qq), waits=[pv])
                    psf_free = [att_evac]
                kv_reader[kslot] = [last_score_mm, pv]
                last_kv_reader = [last_score_mm, pv]
                q_reader[qslot] = last_score_mm
            for bn in range(4):
                psrel(PS4[bn], bank_free[bn])
            psrel(PSE[0], pse_free[0]); psrel(PSE[1], pse_free[1])
            psrel(PSF, psf_free)
            wbuf_owner[0] = last_kv_reader[1]
            return att_evac

        q_ready0 = [rope_stores[2 * NH - 1] for h in range(NH)]
        att0_done = emit_attention(0, [t_kv0, cp_k0, cp_v0], q_ready0, SCALE_ATT,
                                   [wbuf_owner[0]])

        resid_evacs = []

        def evac_wo(bias_sb):
            def evac(g, m, th, pt, t_mm):
                co = g * 4 + m
                sl = co * T + th * 512
                d = sc.op("DV", (lambda sl, pt, co: lambda e: e.scalar_tensor_tensor(
                    resid[:, sl:sl + 512], pt[:], bias_sb[:, co:co + 1],
                    resid[:, sl:sl + 512], ALU.add, ALU.add))(sl, pt, co),
                    waits=[t_mm])
                resid_evacs.append(d)
                return [d]
            return evac

        wo_last = mm_stream(
            wsel=lambda g, kk: wo_bf[g * 512:(g + 1) * 512, kk * 128:(kk + 1) * 128],
            transpose=True, order=list(range(4)), nk=16,
            rhs_ap=lambda kk, th: xnatt[:, kk * T + th * 512:kk * T + th * 512 + 512],
            evac=evac_wo(bo_sb), w_waits=[cast_ops["wo"]], rhs_waits=[att0_done])
        t_resid0 = resid_evacs[-1]

        def dump(name, src_ap, waits):
            if name in dbg_t:
                dst = dbg_t[name]
                if name.startswith("resid"):
                    dst = dst.rearrange("(c p) t -> p c t", p=128)
                    src_ap = resid.ap().rearrange("p (c t) -> p c t", c=NCH)
                elif name == "xn3":
                    dst = dst.rearrange("(c p) t -> p c t", p=128)
                    src_ap = xnatt.ap().rearrange("p (c t) -> p c t", c=NCH)
                return sc.op("SE", (lambda s_, d_: lambda e: e.dma_start(
                    out=d_, in_=s_))(src_ap, dst), waits=waits, q="QA")
            return None

        dump("resid0", None, [t_resid0])
        dump("q0", qdram, [rope_stores[2 * NH - 1]])
        fin_src = t_resid0

        gu_last = wo_last
        if nsub >= 2:
            t_xn2 = emit_ln([wo_last])
            t_quant_rdy = poll_flag(1, waits=[f_quant])
            gu_order = []
            for j in range(16):
                gu_order += [j, 16 + j]
            gu_state = {"i": 0, "sil": {}, "hfree": {}}
            hsts = {}
            stats_started = [False, False]
            stats_mm = {0: None, 1: None}

            def evac_gu(g, m, th, pt, t_mm):
                if g < 16:
                    s8 = m * 2 + th
                    w = [t_mm]
                    prev = gu_state["sil"].get(("rd", s8))
                    if prev:
                        w.append(prev)
                    sil = sc.op("SE", (lambda s8, pt: lambda e: e.activation(
                        sgring[:, s8 * 512:(s8 + 1) * 512], pt[:], AF.Silu))(s8, pt),
                        waits=w)
                    gu_state["sil"][s8] = sil
                    return [sil]
                s8 = m * 2 + th
                s4 = s8 % 4
                jc = (g - 16) * 4 + m
                w = [t_mm, gu_state["sil"][s8]]
                prevh = gu_state["hfree"].get(s4)
                if prevh:
                    w += prevh
                dvh = sc.op("DV", (lambda s8, s4, pt: lambda e: e.tensor_mul(
                    hring[:, s4 * 512:(s4 + 1) * 512],
                    sgring[:, s8 * 512:(s8 + 1) * 512], pt[:]))(s8, s4, pt),
                    waits=w)
                gu_state["sil"][("rd", s8)] = dvh
                sq = sc.op("SE", (lambda s4: lambda e: e.activation(
                    hsqring[:, s4 * 512:(s4 + 1) * 512],
                    hring[:, s4 * 512:(s4 + 1) * 512], AF.Square))(s4),
                    waits=[dvh])
                pstat = PSG if th == 0 else PSF
                first = not stats_started[th]
                stats_started[th] = True
                smw = [sq]
                if first:
                    smw += pswait(pstat)
                smm = sc.op("PE", (lambda pstat, first, s4, jc: lambda e: e.matmul(
                    pstat[0:1, :], onescol[:],
                    hsqring[:, s4 * 512:(s4 + 1) * 512],
                    start=first, stop=(jc == 63)))(pstat, first, s4, jc),
                    waits=smw)
                stats_mm[th] = smm
                hst = sc.op("SE", (lambda jc, th, s4: lambda e: e.dma_start(
                    out=hdram[jc, :, th * 512:th * 512 + 512],
                    in_=hring[:, s4 * 512:(s4 + 1) * 512]))(jc, th, s4),
                    waits=[sq], q="QA")
                hsts[(jc, th)] = hst
                gu_state["hfree"][s4] = [hst, smm]
                return [dvh]

            gu_last = mm_stream(
                wsel=lambda g, kk: sh_wgu[g * 512:(g + 1) * 512,
                                          kk * 128:(kk + 1) * 128],
                transpose=True, order=gu_order, nk=16,
                rhs_ap=lambda kk, th: xnatt[:, kk * T + th * 512:kk * T + th * 512 + 512],
                evac=evac_gu, w_waits=[t_quant_rdy, cp_wgu], rhs_waits=[t_xn2])

            rstd_cp = {}
            for th in range(2):
                pstat = PSG if th == 0 else PSF
                ms = srow[0:1, 0:512] if th == 0 else srow[0:1, 512:1024]
                e1 = sc.op("DV", (lambda ms, pstat: lambda e: e.tensor_scalar(
                    ms, pstat[0:1, :], 1.0 / INTER, RMS_EPS, ALU.mult, ALU.add))(
                    ms, pstat), waits=[stats_mm[th]])
                e2 = sc.op("DV", (lambda ms: lambda e: e.reciprocal(ms, ms))(ms),
                           waits=[e1])
                e3 = sc.op("SE", (lambda ms: lambda e: e.activation(
                    ms, ms, AF.Sqrt))(ms), waits=[e2])
                pb = PSC if th == 0 else PSD
                e4 = sc.op("PE", (lambda ms, pb: lambda e: e.matmul(
                    pb[:], onesrow[:], ms, start=True, stop=True))(ms, pb),
                    waits=[e3] + pswait(pb))
                e5 = sc.op("DV", (lambda th, pb: lambda e: e.tensor_copy(
                    rstd_sb[:, th * 512:th * 512 + 512], pb[:]))(th, pb),
                    waits=[e4])
                psrel(pb, [e5])
                rstd_cp[th] = e5
            psrel(PSG, [rstd_cp[0]]); psrel(PSF, [rstd_cp[1]])

            down_last = None
            hload_prev = {}
            dw_prev = {}
            for th in range(2):
                for dg in range(4):
                    bank_w = {m: pswait(PS4[m]) for m in range(4)}
                    mm = None
                    for sp in range(4):
                        wb = sp % 2
                        lw = [t_quant_rdy, cp_wp]
                        if (th, dg, sp) == (0, 0, 0):
                            lw += [gu_last, wbuf_owner[0], wbuf_owner[1]]
                        prev = dw_prev.get(wb)
                        if prev:
                            lw.append(prev)
                        wlo = None
                        for kk in range(16):
                            jc = sp * 16 + kk
                            wlo = sc.op("SP", (lambda dg, jc, wb, kk: lambda e: e.dma_start(
                                out=wbuf[wb][:, kk * 512:(kk + 1) * 512],
                                in_=sh_wp[dg * 512:(dg + 1) * 512,
                                          jc * 128:(jc + 1) * 128],
                                transpose=True))(dg, jc, wb, kk),
                                waits=(lw if kk == 0 else []), q="QS")
                        for kk in range(16):
                            jc = sp * 16 + kk
                            hslot = jc % 4
                            hw = [hsts[(jc, th)]]
                            prevh = hload_prev.get(hslot)
                            if prevh:
                                hw.append(prevh)
                            hl = sc.op("SE", (lambda jc, th, hslot: lambda e: e.dma_start(
                                out=kring[:, hslot * 512:(hslot + 1) * 512],
                                in_=hdram[jc, :, th * 512:th * 512 + 512]))(
                                jc, th, hslot), waits=hw, q="QA")
                            for m in range(4):
                                w = [wlo, hl]
                                if sp == 0 and kk == 0:
                                    w += bank_w[m]
                                mm = sc.op("PE", (lambda m, kk, wb, hslot, sp: lambda e: e.matmul(
                                    PS4[m][:],
                                    wbuf[wb][:, kk * 512 + m * 128:kk * 512 + (m + 1) * 128],
                                    kring[:, hslot * 512:(hslot + 1) * 512],
                                    start=(sp == 0 and kk == 0),
                                    stop=(sp == 3 and kk == 15)))(m, kk, wb, hslot, sp),
                                    waits=(w if kk == 0 or (sp == 0 and kk == 0)
                                           else [hl]))
                            hload_prev[hslot] = mm
                        dw_prev[wb] = mm
                    for m in range(4):
                        co = dg * 4 + m
                        sl = co * T + th * 512
                        da = sc.op("DV", (lambda m, th: lambda e: e.tensor_mul(
                            tmp1[:, 512:1024], PS4[m][:],
                            rstd_sb[:, th * 512:th * 512 + 512]))(m, th),
                            waits=[mm, rstd_cp[th]])
                        db = sc.op("DV", (lambda sl: lambda e: e.tensor_add(
                            resid[:, sl:sl + 512], resid[:, sl:sl + 512],
                            tmp1[:, 512:1024]))(sl), waits=[da])
                        psrel(PS4[m], [da])
                        down_last = db
            wbuf_owner[0] = dw_prev.get(0, wbuf_owner[0])
            wbuf_owner[1] = dw_prev.get(1, wbuf_owner[1])
            t_resid1 = down_last
            dump("resid1", None, [t_resid1])
            fin_src = t_resid1

        if nsub >= 4:
            spl_store_last = None
            prev_cpw_reader = None
            for w in "gud":
                Oh = {"g": 4096, "u": 4096, "d": 1024}[w]
                ld_cpw = sc.op("PO", (lambda w: lambda e: e.dma_start(
                    out=cpw_sb.ap().rearrange("p (d f) -> p d f", d=5),
                    in_=cpwI[w].rearrange("d p f -> p d f")))(w),
                    waits=[prev_cpw_reader], q="QGc6")
                coef_sts = []
                gmul_prev = {0: None, 1: None}
                for c in range(8):
                    taps = 4 if c < 4 else 5
                    srcg = gaI[w] if c < 4 else gdI[w]
                    ci = c if c < 4 else c - 4
                    last_add = None
                    for dd in range(taps):
                        gslot = dd % 2
                        gw = []
                        if gmul_prev[gslot]:
                            gw.append(gmul_prev[gslot])
                        if dd == 0 and coef_sts:
                            gw.append(coef_sts[-1])
                        gl = sc.op("PO", (lambda srcg, ci, dd, gslot: lambda e: e.dma_start(
                            out=gbuf[:, gslot * 512:(gslot + 1) * 512],
                            in_=srcg[ci, dd]))(srcg, ci, dd, gslot),
                            waits=gw, q=f"QG{2 + gslot}")
                        if dd == 0:
                            mo = sc.op("DV", (lambda dd, gslot: lambda e: e.tensor_mul(
                                coefacc, cpw_sb[:, dd * 512:(dd + 1) * 512],
                                gbuf[:, gslot * 512:(gslot + 1) * 512]))(dd, gslot),
                                waits=[gl, ld_cpw])
                        else:
                            m1 = sc.op("DV", (lambda dd, gslot: lambda e: e.tensor_mul(
                                gtmp, cpw_sb[:, dd * 512:(dd + 1) * 512],
                                gbuf[:, gslot * 512:(gslot + 1) * 512]))(dd, gslot),
                                waits=[gl])
                            mo = sc.op("DV", lambda e: e.tensor_add(
                                coefacc, coefacc, gtmp), waits=[m1])
                        gmul_prev[gslot] = mo
                        last_add = mo
                    if c < 4:
                        hi = sc.op("DV", lambda e: e.tensor_copy(
                            coefbf[:], coefacc), waits=[last_add])
                        lo = sc.op("DV", lambda e: e.tensor_sub(
                            coefbf2[:], coefacc, coefbf[:]), waits=[hi])
                        gmul_prev[0] = lo
                        gmul_prev[1] = lo
                        coef_sts.append(sc.op("PO", (lambda w, c: lambda e: e.dma_start(
                            out=coefd[w][c:c + 1, :].rearrange(
                                "o (p f) -> (o p) f", p=128),
                            in_=coefbf[:]))(w, c), waits=[hi], q="QGst"))
                        coef_sts.append(sc.op("PO", (lambda w, c: lambda e: e.dma_start(
                            out=coefd[w][4 + c:5 + c, :].rearrange(
                                "o (p f) -> (o p) f", p=128),
                            in_=coefbf2[:]))(w, c), waits=[lo], q="QGst"))
                    else:
                        hi = sc.op("DV", lambda e: e.tensor_copy(
                            coefbf[:], coefacc), waits=[last_add])
                        gmul_prev[0] = hi
                        gmul_prev[1] = hi
                        coef_sts.append(sc.op("PO", (lambda w, c: lambda e: e.dma_start(
                            out=coefd[w][4 + c:5 + c, :].rearrange(
                                "o (p f) -> (o p) f", p=128),
                            in_=coefbf[:]))(w, c), waits=[hi], q="QGst"))
                prev_cpw_reader = coef_sts[-1]
                mmA_prev = {0: None, 1: None}
                mmD_prev = {0: None, 1: None}
                md_prev = {0: None, 1: None}
                wsub_prev = {"any": None}
                eca = pswait(PSC)
                ecd = pswait(PSD)
                for tau in range(128):
                    slot = tau % 2
                    rw = [coef_sts[-1]]
                    if mmA_prev[slot]:
                        rw.append(mmA_prev[slot])
                    rl = sc.op("PO", (lambda w, tau, slot: lambda e: e.dma_start(
                        out=evalrhs[0:8, slot * 512:(slot + 1) * 512],
                        in_=coefd[w][0:8, tau * 512:(tau + 1) * 512]))(w, tau, slot),
                        waits=rw, q=f"QG{4 + slot}")
                    rw2 = [coef_sts[-1]]
                    if mmD_prev[slot]:
                        rw2.append(mmD_prev[slot])
                    rld = sc.op("PO", (lambda w, tau, slot: lambda e: e.dma_start(
                        out=evalrhsD[0:4, slot * 512:(slot + 1) * 512],
                        in_=coefd[w][8:12, tau * 512:(tau + 1) * 512]))(w, tau, slot),
                        waits=rw2, q=f"QG{10 + slot}")
                    mw = []
                    if md_prev[slot]:
                        mw.append(md_prev[slot])
                    ml = sc.op("PO", (lambda w, tau, slot: lambda e: e.dma_start(
                        out=emask[:, slot * 512:(slot + 1) * 512],
                        in_=mskI[w][tau]))(w, tau, slot), waits=mw,
                        q=f"QG{6 + slot}")
                    mmA = sc.op("PE", (lambda slot: lambda e: e.matmul(
                        PSC[:], ramp4_sb[:],
                        evalrhs[0:8, slot * 512:(slot + 1) * 512],
                        start=True, stop=True))(slot),
                        waits=[rl, t_consts] + eca)
                    mmA_prev[slot] = mmA
                    mmD = sc.op("PE", (lambda slot: lambda e: e.matmul(
                        PSD[:], ramp4_sb[0:4, :],
                        evalrhsD[0:4, slot * 512:(slot + 1) * 512],
                        start=True, stop=True))(slot), waits=[rld] + ecd)
                    mmD_prev[slot] = mmD
                    md = sc.op("DV", (lambda slot: lambda e: e.tensor_mul(
                        mdtmp, emask[:, slot * 512:(slot + 1) * 512],
                        PSD[:]))(slot), waits=[mmD, ml])
                    md_prev[slot] = md
                    ecd = [md]
                    ww = [mmA]
                    if wsub_prev["any"]:
                        ww.append(wsub_prev["any"])
                    else:
                        ww.append(q_store_last)
                    ws_ = sc.op("DV", (lambda slot: lambda e: e.tensor_sub(
                        wout[:, 0:512], PSC[:], mdtmp))(slot),
                        waits=ww)
                    eca = [ws_]
                    c0 = tau // (Oh // 512)
                    og = tau % (Oh // 512)

                    def emit_wst(e, w=w, c0=c0, og=og):
                        return e.dma_start(
                            out=sploc[w][c0 * 128:(c0 + 1) * 128,
                                         og * 512:(og + 1) * 512],
                            in_=wout[:, 0:512])
                    wst = sc.op("PO", emit_wst, waits=[ws_],
                                q=f"QG{8 + slot}")
                    wsub_prev["any"] = wst
                    spl_store_last = wst
                psrel(PSC, eca)
                psrel(PSD, ecd)
            cp_spl = {}
            for w2 in "gud":
                if w2 == "d":
                    cp_spl[w2] = shared_copy(
                        lambda r, w2=w2: sh_spl[w2].rearrange(
                            "i (r o) -> i r o", r=2)[:, bass.ts(r, 1), :].squeeze(1),
                        sploc[w2], [spl_store_last])
                else:
                    cp_spl[w2] = shared_copy(
                        lambda r, w2=w2: sh_spl[w2].rearrange(
                            "h (r o) -> h r o", r=2)[:, bass.ts(r, 1), :].squeeze(1),
                        sploc[w2], [spl_store_last])
            f_spline = publish_flag(2, [cp_spl["g"], cp_spl["u"], cp_spl["d"]])

        if nsub >= 3:
            t_xn3 = emit_ln([gu_last])
            dump("xn3", None, [t_xn3])
            qk1_store = []
            st1 = {"i": 0}

            def evac_qkv1(g, m, th, pt, t_mm):
                i = st1["i"]; st1["i"] += 1
                kind = "q" if g < 4 else "k"
                co = (g % 4) * 4 + m
                ring = qring if kind == "q" else kring
                slot = i % 4
                dsl = ring[:, slot * 512:(slot + 1) * 512]
                boff = (0 if kind == "q" else 16) + co
                w = [t_mm]
                if i >= 4:
                    w.append(qk1_store[i - 4])
                se = sc.op("SE", (lambda dsl, pt, boff: lambda e: e.activation(
                    dsl, pt[:], AF.Identity, bias=bqkv_sb[:, boff:boff + 1]))(
                    dsl, pt, boff), waits=w)
                if kind == "q":
                    qk1_store.append(sc.op("SE", (lambda co, th, dsl: lambda e: e.dma_start(
                        out=qdram[co * 128:(co + 1) * 128, th * 512:th * 512 + 512],
                        in_=dsl))(co, th, dsl), waits=[se], q="QA"))
                else:
                    qk1_store.append(sc.op("SE", (lambda co, th, dsl: lambda e: e.dma_start(
                        out=kloc1[co * 128:(co + 1) * 128,
                                  th * 512:th * 512 + 512],
                        in_=dsl))(co, th, dsl), waits=[se], q="QA"))
                return [se]

            qkv1_last = mm_stream(
                wsel=lambda g, kk: wqkv_bf[kk * 128:(kk + 1) * 128,
                                           g * 512:(g + 1) * 512],
                transpose=False, order=list(range(8)), nk=16,
                rhs_ap=lambda kk, th: xnatt[:, kk * T + th * 512:kk * T + th * 512 + 512],
                evac=evac_qkv1, w_waits=[cast_ops["wqkv"]], rhs_waits=[t_xn3])

            v1_stores = []
            v1_mm_last = {}
            prev_v1mm = None
            for dgv in range(4):
                bvld = sc.op("SE", (lambda dgv: lambda e: e.dma_start(
                    out=bvrow_sb[:],
                    in_=bqkvv_row[0:1, dgv * 512:(dgv + 1) * 512]))(dgv),
                    waits=([v1_stores[-1]] if v1_stores else []), q="QA")
                wvl = None
                for kk in range(16):
                    if kk == 0:
                        if dgv < 2:
                            lw = [qkv1_last, wbuf_owner[dgv % 2]]
                        else:
                            lw = [v1_mm_last[dgv - 2]]
                    else:
                        lw = []
                    wvl = sc.op("SP", (lambda dgv, kk: lambda e: e.dma_start(
                        out=wbuf[dgv % 2][:, kk * 512:(kk + 1) * 512],
                        in_=wqkv_bf[kk * 128:(kk + 1) * 128,
                                    4096 + dgv * 512:4096 + (dgv + 1) * 512]))(dgv, kk),
                        waits=lw, q="QS")
                bmm = sc.op("PE", lambda e: e.matmul(
                    PSD[:], onesrow[:], bvrow_sb[0:1, :], start=True, stop=True),
                    waits=[t_consts, bvld] + pswait(PSD))
                bcp = sc.op("DV", lambda e: e.tensor_copy(bvb_sb[:], PSD[:]),
                            waits=[bmm])
                psrel(PSD, [bcp])
                vfree = {0: pswait(PSC), 1: pswait(PSD)}
                mmv1 = None
                for ti in range(8):
                    w = [wvl, bcp] + vfree[ti % 2]
                    for kk in range(16):
                        mmv1 = sc.op("PE", (lambda dgv, ti, kk: lambda e: e.matmul(
                            PS4[2 + ti % 2][:],
                            xnatt[:, kk * T + ti * 128:kk * T + (ti + 1) * 128],
                            wbuf[dgv % 2][:, kk * 512:(kk + 1) * 512],
                            start=(kk == 0), stop=(kk == 15)))(dgv, ti, kk),
                            waits=(w if kk == 0 else []))
                    wv2 = [mmv1]
                    if len(v1_stores) >= 2:
                        wv2.append(v1_stores[-2])
                    dvv = sc.op("DV", (lambda ti: lambda e: e.tensor_add(
                        vring[:, (ti % 2) * 512:(ti % 2) * 512 + 512],
                        PS4[2 + ti % 2][:], bvb_sb[:]))(ti), waits=wv2)
                    vfree[ti % 2] = [dvv]
                    v1_stores.append(sc.op("SE", (lambda dgv, ti: lambda e: e.dma_start(
                        out=vloc1[ti * 128:(ti + 1) * 128,
                                  dgv * 512:(dgv + 1) * 512],
                        in_=vring[:, (ti % 2) * 512:(ti % 2) * 512 + 512]))(dgv, ti),
                        waits=[dvv], q="QA"))
                v1_mm_last[dgv] = mmv1
                psrel(PSC, vfree[0]); psrel(PSD, vfree[1])
            wbuf_owner[0] = v1_mm_last[2]
            wbuf_owner[1] = v1_mm_last[3]
            if STOP_AFTER_QKV1:
                yst0 = sc.op("SE", lambda e: e.dma_start(
                    out=yout.rearrange("(c p) t -> p c t", p=128),
                    in_=resid.ap().rearrange("p (c t) -> p c t", c=NCH)),
                    waits=[fin_src, qk1_store[-1], v1_stores[-1]], q="QA")
                sc.op("PO", lambda e: e.engine_nop(), waits=[yst0])
                sc.finalize_and_emit(block)
                return nc
            cp_k1 = shared_copy(
                lambda r: sh_k1[bass.ts(r, 1)].squeeze(0), kloc1,
                [qk1_store[-1]])
            cp_v1 = shared_copy(
                lambda r: sh_v1[bass.ts(r, 1)].squeeze(0), vloc1,
                [v1_stores[-1]])
            f_kv1 = publish_flag(3, [cp_k1, cp_v1])
            t_kv1 = poll_flag(3, waits=[f_kv1])
            q_ready1 = [qk1_store[2 * NH - 1] for h in range(NH)]
            att1_done = emit_attention(1, [t_kv1, cp_k1, cp_v1], q_ready1,
                                       SCALE_ATT,
                                       [wbuf_owner[0], wbuf_owner[1]])
            wo1_last = mm_stream(
                wsel=lambda g, kk: wo1_bf[kk * 128:(kk + 1) * 128,
                                          g * 512:(g + 1) * 512],
                transpose=False, order=list(range(4)), nk=16,
                rhs_ap=lambda kk, th: xnatt[:, kk * T + th * 512:kk * T + th * 512 + 512],
                evac=evac_wo(bo1_sb), w_waits=[cast_ops["wo1"]],
                rhs_waits=[att1_done])
            t_resid2 = resid_evacs[-1]
            dump("resid2", None, [t_resid2])
            fin_src = t_resid2

        if nsub >= 4:
            t_xn4 = emit_ln([wo1_last])
            t_spl_rdy = poll_flag(2)
            gu1_state = {"sil": {}}
            inter_sts = {}
            i1sts = {}

            def evac_gu1(g, m, th, pt, t_mm):
                which, j = g % 2, g // 2
                s8 = m * 2 + th
                if which == 0:
                    w = [t_mm]
                    prev = gu1_state["sil"].get(("rd", s8))
                    if prev:
                        w.append(prev)
                    sil = sc.op("SE", (lambda s8, pt: lambda e: e.activation(
                        sgring[:, s8 * 512:(s8 + 1) * 512], pt[:], AF.Silu))(s8, pt),
                        waits=w)
                    gu1_state["sil"][s8] = sil
                    return [sil]
                s4 = s8 % 4
                jc = j * 4 + m
                w = [t_mm, gu1_state["sil"][s8]]
                prevh = inter_sts.get(s4)
                if prevh:
                    w.append(prevh)
                dvh = sc.op("DV", (lambda s8, s4, pt: lambda e: e.tensor_mul(
                    hring[:, s4 * 512:(s4 + 1) * 512],
                    sgring[:, s8 * 512:(s8 + 1) * 512], pt[:]))(s8, s4, pt),
                    waits=w)
                gu1_state["sil"][("rd", s8)] = dvh
                hst = sc.op("SE", (lambda jc, th, s4: lambda e: e.dma_start(
                    out=hdram[jc, :, th * 512:th * 512 + 512],
                    in_=hring[:, s4 * 512:(s4 + 1) * 512]))(jc, th, s4),
                    waits=[dvh], q="QA")
                inter_sts[s4] = hst
                i1sts[(jc, th)] = hst
                return [dvh]

            gu1_order = list(range(32))

            def wsel_gu1(g, kk):
                which, j = g % 2, g // 2
                srcw = sh_spl["g"] if which == 0 else sh_spl["u"]
                return srcw[kk * 128:(kk + 1) * 128, j * 512:(j + 1) * 512]

            gu1_last = mm_stream(
                wsel=wsel_gu1, transpose=False, order=gu1_order, nk=16,
                rhs_ap=lambda kk, th: xnatt[:, kk * T + th * 512:kk * T + th * 512 + 512],
                evac=evac_gu1, w_waits=[t_spl_rdy, cp_spl["g"], cp_spl["u"]], rhs_waits=[t_xn4])

            down1_last = None
            hload_prev = {}
            dw_prev = {}
            for th in range(2):
                for dg in range(4):
                    bank_w = {m: pswait(PS4[m]) for m in range(4)}
                    mm = None
                    for sp in range(4):
                        wb = sp % 2
                        lw = [t_spl_rdy, cp_spl["d"]]
                        if (th, dg, sp) == (0, 0, 0):
                            lw += [gu1_last, wbuf_owner[0], wbuf_owner[1]]
                        prev = dw_prev.get(wb)
                        if prev:
                            lw.append(prev)
                        wlo = None
                        for kk in range(16):
                            jc = sp * 16 + kk
                            wlo = sc.op("SP", (lambda dg, jc, wb, kk: lambda e: e.dma_start(
                                out=wbuf[wb][:, kk * 512:(kk + 1) * 512],
                                in_=sh_spl["d"][jc * 128:(jc + 1) * 128,
                                                dg * 512:(dg + 1) * 512]))(dg, jc, wb, kk),
                                waits=(lw if kk == 0 else []), q="QS")
                        for kk in range(16):
                            jc = sp * 16 + kk
                            hslot = jc % 4
                            hw = [i1sts[(jc, th)]]
                            prevh = hload_prev.get(hslot)
                            if prevh:
                                hw.append(prevh)
                            hl = sc.op("SE", (lambda jc, th, hslot: lambda e: e.dma_start(
                                out=kring[:, hslot * 512:(hslot + 1) * 512],
                                in_=hdram[jc, :, th * 512:th * 512 + 512]))(
                                jc, th, hslot), waits=hw, q="QA")
                            for m in range(4):
                                w = [wlo, hl]
                                if sp == 0 and kk == 0:
                                    w += bank_w[m]
                                mm = sc.op("PE", (lambda m, kk, wb, hslot, sp: lambda e: e.matmul(
                                    PS4[m][:],
                                    wbuf[wb][:, kk * 512 + m * 128:kk * 512 + (m + 1) * 128],
                                    kring[:, hslot * 512:(hslot + 1) * 512],
                                    start=(sp == 0 and kk == 0),
                                    stop=(sp == 3 and kk == 15)))(m, kk, wb, hslot, sp),
                                    waits=(w if kk == 0 or (sp == 0 and kk == 0)
                                           else [hl]))
                            hload_prev[hslot] = mm
                        dw_prev[wb] = mm
                    for m in range(4):
                        co = dg * 4 + m
                        sl = co * T + th * 512
                        db = sc.op("DV", (lambda sl, m: lambda e: e.tensor_add(
                            resid[:, sl:sl + 512], resid[:, sl:sl + 512],
                            PS4[m][:]))(sl, m), waits=[mm])
                        psrel(PS4[m], [db])
                        down1_last = db
            wbuf_owner[0] = dw_prev.get(0, wbuf_owner[0])
            wbuf_owner[1] = dw_prev.get(1, wbuf_owner[1])
            fin_src = down1_last

        yst = sc.op("SE", lambda e: e.dma_start(
            out=yout.rearrange("(c p) t -> p c t", p=128),
            in_=resid.ap().rearrange("p (c t) -> p c t", c=NCH)),
            waits=[fin_src], q="QA")
        sc.op("PO", lambda e: e.engine_nop(), waits=[yst])

        sc.finalize_and_emit(block)
    return nc


def _shard_inputs(inputs):
    rng = np.random.default_rng()
    magic = rng.integers(1, 1 << 30, size=(4, 2 * NFLAGS), dtype=np.int32)
    x = np.asarray(inputs["L0_attn_x"], np.float32)
    in_maps = []
    for core in range(8):
        b = core // 2
        role = core % 2
        cst = host_constants(role)
        im = {}
        tl = np.arange(T)
        gpos = (2 * (tl // 128) + role) * 128 + tl % 128
        im["xT"] = np.ascontiguousarray(x[b][gpos].T)
        im["wq"] = np.ascontiguousarray(inputs["L0_attn_Wq"], np.float32)
        im["wk"] = np.ascontiguousarray(inputs["L0_attn_Wk"], np.float32)
        im["wv"] = np.ascontiguousarray(inputs["L0_attn_Wv"], np.float32)
        im["wo"] = np.ascontiguousarray(inputs["L0_attn_Wo"], np.float32)
        im["bq_c"] = np.ascontiguousarray(
            np.asarray(inputs["L0_attn_bq"], np.float32).reshape(NCH, 128).T)
        im["bk_c"] = np.ascontiguousarray(
            np.asarray(inputs["L0_attn_bk"], np.float32).reshape(NKV, 128).T)
        im["bv_row"] = np.asarray(inputs["L0_attn_bv"], np.float32)[None, :]
        im["bo_c"] = np.ascontiguousarray(
            np.asarray(inputs["L0_attn_bo"], np.float32).reshape(NCH, 128).T)
        wgu = np.asarray(inputs["L0_ffn_w_gate_up"], np.float32)
        im["wgu_sl"] = np.ascontiguousarray(wgu[role * FFH:(role + 1) * FFH])
        wp = np.asarray(inputs["L0_ffn_w_proj"], np.float32)
        im["wp_sl"] = np.ascontiguousarray(
            wp[role * (H // 2):(role + 1) * (H // 2)])
        im["wqkv"] = np.ascontiguousarray(inputs["L1_attn_W_qkv"], np.float32)
        bqkv = np.asarray(inputs["L1_attn_b_qkv"], np.float32)
        bq1 = bqkv[0:H].reshape(NCH, 128).T
        bk1 = bqkv[H:2 * H].reshape(NCH, 128).T
        im["bqkv_c"] = np.ascontiguousarray(np.concatenate([bq1, bk1], 1))
        im["bqkvv_row"] = np.ascontiguousarray(bqkv[2 * H:3 * H][None, :])
        im["wo1"] = np.ascontiguousarray(inputs["L1_attn_W_o"], np.float32)
        im["bo1_c"] = np.ascontiguousarray(
            np.asarray(inputs["L1_attn_b_o"], np.float32).reshape(NCH, 128).T)
        for w, cpname, O in (("g", "L1_ffn_gate_cp", INTER),
                             ("u", "L1_ffn_up_cp", INTER),
                             ("d", "L1_ffn_down_cp", H)):
            sm = spline_meta(role, O)
            cp = np.asarray(inputs[cpname], np.float32)
            im[f"cpw_{w}"] = np.ascontiguousarray(
                cp[sm["idx"]].reshape(5, 128, 512)).astype(np.float32)
            im[f"ga_{w}"] = sm["GA"]
            im[f"gd_{w}"] = sm["GD"]
            im[f"msk_{w}"] = sm["mask"]
        for k in ("identI", "ones_colI", "ones_rowI", "cos_I", "sin_I",
                  "rmatTI", "dmaskI", "ramp4I"):
            im[k] = cst[k]
        im["roleI"] = np.array([[role, 1 - role]], dtype=np.int32)
        im["magicI"] = magic[b][None, :]
        in_maps.append(im)
    return in_maps


_NC_CACHE = {}


def kernel(**inputs):
    key = (N_SUBBLOCKS, tuple(DBG))
    if key not in _NC_CACHE:
        _NC_CACHE[key] = build_program(N_SUBBLOCKS, DBG)
    nc = _NC_CACHE[key]
    in_maps = _shard_inputs(inputs)
    res = run_bass_kernel_spmd(nc, in_maps, core_ids=list(range(8)))
    out = np.empty((B, S, H), np.float32)
    tl = np.arange(T)
    for core in range(8):
        b, role = core // 2, core % 2
        gpos = (2 * (tl // 128) + role) * 128 + tl % 128
        y = np.asarray(res.results[core]["y"])
        out[b][gpos] = y.T
    return out
